# version 11
# baseline (speedup 1.0000x reference)
import numpy as np
import ml_dtypes
import concourse.bass as bass
import concourse.mybir as mybir
from concourse.bass_utils import run_bass_kernel_spmd

F32 = mybir.dt.float32
BF16 = mybir.dt.bfloat16
AF = mybir.ActivationFunctionType
ALU = mybir.AluOpType
AX = mybir.AxisListType

D = 1024
C8 = 8
NE = 32
NU = 34
I_HALF = 512
NEG = -1.0e30


class Sem:
    def __init__(self, h):
        self.h = h
        self.n = 0


class Builder:
    def __init__(self):
        self.q = {k: [] for k in ("tensor", "vector", "scalar", "gpsimd", "sync")}

    def emit(self, eng, fn, waits=(), inc=None, k=1):
        wl = tuple((s.h, int(v)) for (s, v) in waits if v > 0)
        ih = inc.h if inc is not None else None

        def g(e, wl=wl, fn=fn, ih=ih, k=k):
            for h, v in wl:
                e.wait_ge(h, v)
            r = fn(e)
            if ih is not None:
                r.then_inc(ih, k)
        self.q[eng].append(g)


def build_core(T, dbg=False):
    NT = T // 256
    NQ = T // 1024
    nc = bass.Bass("TRN2", target_bir_lowering=False)

    io = "ExternalOutput" if dbg else "Internal"
    x_h = nc.dram_tensor("x_fm", [D, T], F32, kind="ExternalInput").ap()
    wq_h = nc.dram_tensor("wq", [D, D], F32, kind="ExternalInput").ap()
    wk_h = nc.dram_tensor("wk", [D, D], F32, kind="ExternalInput").ap()
    wv_h = nc.dram_tensor("wv", [D, D], F32, kind="ExternalInput").ap()
    wfc_h = nc.dram_tensor("wfc", [D, D], F32, kind="ExternalInput").ap()
    wr_h = nc.dram_tensor("wr_t", [D, NE], F32, kind="ExternalInput").ap()
    gam_h = nc.dram_tensor("gamma2", [128, C8], F32, kind="ExternalInput").ap()
    bet_h = nc.dram_tensor("beta2", [128, C8], F32, kind="ExternalInput").ap()
    rb_h = nc.dram_tensor("rbias", [1, NE], F32, kind="ExternalInput").ap()
    gw_h = nc.dram_tensor("gw_all", [NU, D, I_HALF], F32, kind="ExternalInput").ap()
    uw_h = nc.dram_tensor("uw_all", [NU, D, I_HALF], F32, kind="ExternalInput").ap()
    dw_h = nc.dram_tensor("dw_all", [NU, I_HALF, D], F32, kind="ExternalInput").ap()
    onesb_h = nc.dram_tensor("ones_col", [128, 1], BF16, kind="ExternalInput").ap()
    onesr_h = nc.dram_tensor("ones_row", [1, 128], F32, kind="ExternalInput").ap()
    onesrb_h = nc.dram_tensor("ones_row_bf", [1, 128], BF16, kind="ExternalInput").ap()
    e2_h = nc.dram_tensor("e2", [2, 128], F32, kind="ExternalInput").ap()
    id_h = nc.dram_tensor("id128", [128, 128], BF16, kind="ExternalInput").ap()

    out_h = nc.dram_tensor("out_fm", [D, T], F32, kind="ExternalOutput").ap()
    xn_scr = nc.dram_tensor("xn_scr", [D, T], BF16, kind=io).ap()
    q_scr = nc.dram_tensor("q_scr", [D, T], BF16, kind=io).ap()
    w_scr = nc.dram_tensor("w_scr", [NE, T], BF16, kind=io).ap()

    from contextlib import ExitStack
    stack = ExitStack()
    e = stack.enter_context

    wbuf = e(nc.sbuf_tensor("wbuf", [128, 24576], BF16))
    wfc_s = e(nc.sbuf_tensor("wfc_s", [128, C8, 1024], BF16))
    mixed = e(nc.sbuf_tensor("mixed", [128, C8, 1024], F32))
    xq_bf = e(nc.sbuf_tensor("xq_bf", [128, C8, 1024], BF16))
    gq_bf = e(nc.sbuf_tensor("gq_bf", [128, C8, 1024], BF16))
    kvbd = e(nc.sbuf_tensor("kvbd", [128, C8, 128], BF16))
    ksbd = e(nc.sbuf_tensor("ksbd", [128, C8, 2], BF16))
    gam_s = e(nc.sbuf_tensor("gam_s", [128, C8], F32))
    bet_s = e(nc.sbuf_tensor("bet_s", [128, C8], F32))
    onesb = e(nc.sbuf_tensor("onesb", [128, 1], BF16))
    onesr = e(nc.sbuf_tensor("onesr", [1, 128], F32))
    onesrb = e(nc.sbuf_tensor("onesrb", [1, 128], BF16))
    e2_s = e(nc.sbuf_tensor("e2_s", [2, 128], F32))
    id_s = e(nc.sbuf_tensor("id_s", [128, 128], BF16))
    rb_s = e(nc.sbuf_tensor("rb_s", [1, NE], F32))
    rbb_s = e(nc.sbuf_tensor("rbb_s", [128, NE], F32))
    wr_s = e(nc.sbuf_tensor("wr_s", [128, C8, NE], F32))
    wrhi = e(nc.sbuf_tensor("wrhi", [128, C8, NE], BF16))
    wrlo = e(nc.sbuf_tensor("wrlo", [128, C8, NE], BF16))
    zrow = e(nc.sbuf_tensor("zrow", [1, 512], BF16))

    SF = e(nc.sbuf_tensor("scr_f32", [128, 10240], F32))
    SB = e(nc.sbuf_tensor("scr_bf", [128, 15616], BF16))

    r3 = lambda ap, t: ap.rearrange("p (c t) -> p c t", t=t)
    xA = [r3(SF[:, i * 2048:(i + 1) * 2048], 256) for i in range(2)]
    xn1A = r3(SF[:, 4096:6144], 256)
    xnfA = r3(SF[:, 6144:8192], 256)
    scA = SF[:, 8192:8224]
    sfcA = SF[:, 8224:8256]
    m1A = SF[:, 8256:8260]
    gmkA = SF[:, 8260:8264]
    gm1A = SF[:, 8264:8265]
    gm2A = SF[:, 8265:8266]
    t0A = SF[:, 8266:8298]
    yA = SF[:, 8298:8330]
    tmpA = SF[:, 8330:8362]
    posA = SF[:, 8362:8394]
    pA_ = SF[:, 8394:8395]
    mx8A = SF[:, 8395:8403]
    t4A = SF[:, 8403:8404]
    fmpA = SF[:, 8404:8405]
    cndA = SF[:, 8405:8406]
    ucndA = SF[:, 8406:8407]
    dnmA = SF[:, 8407:8408]
    recA = SF[:, 8408:8409]
    cA = [SF[:, 8409 + i * 32: 8441 + i * 32] for i in range(2)]
    mmainA = SF[:, 8473:8505]
    zA = SF[:, 8505:8537]
    pf = [SF[:, 8537 + i * 32: 8569 + i * 32] for i in range(2)]
    wnumA = SF[:, 8601:8633]
    wtA = SF[:, 8633:8665]
    slA = [SF[0:1, 8960 + i * 256: 9216 + i * 256] for i in range(4)]

    xb2 = [r3(SB[:, 0:2048], 256), r3(SB[:, 13568:15616], 256)]
    xbA = xb2[0]
    x2A = r3(SB[:, 2048:4096], 256)
    xnbA = r3(SB[:, 4096:6144], 256)
    xnlA = r3(SB[:, 6144:8192], 256)
    qstA = r3(SB[:, 8192:10240], 256)
    ktmA = SB[:, 10240:11264]
    vexA = r3(SB[:, 11264:12296], 129)
    wtbA = SB[:, 12296:12328]
    w_all = r3(SB[:, 12544:13568], 32)
    wemA = SB[0:32, 12328:12456]

    sgB = [SB[:, i * 512:(i + 1) * 512] for i in range(2)]
    hv1B = SB[:, 1024:1536]
    hvB = [r3(SB[:, 1536 + i * 2048: 3584 + i * 2048], 512) for i in range(2)]
    wbB = [SB[:, 5632 + i * 1024: 6656 + i * 1024] for i in range(2)]
    evB = [SF[:, i * 512:(i + 1) * 512] for i in range(2)]

    xC = r3(SF[:, 0:2048], 256)
    xatC = r3(SF[:, 2048:4096], 256)
    t1C = r3(SF[:, 4096:6144], 256)
    t2C = r3(SF[:, 6144:8192], 256)
    p1C = t1C
    outC = r3(SF[:, 8192:10240], 256)
    qtC = [r3(SB[:, i * 2048:(i + 1) * 2048], 256) for i in range(2)]
    ndC = r3(SB[:, 4096:6144], 256)
    xabC = r3(SB[:, 6144:8192], 256)
    xsqC = r3(SB[:, 8192:10240], 256)
    sC = r3(SB[:, 10240:12288], 256)
    rdbC = SB[:, 12288:12544]
    slC = [SB[0:1, 12544 + i * 512: 13056 + i * 512].bitcast(F32) for i in range(4)]
    rdC = SB[0:2, 12544:13056].bitcast(F32)
    rd2C = SB[0:2, 13056:13568].bitcast(F32)
    rdC2b = SB[0:2, 13568:14080].bitcast(F32)
    rd2C2b = SB[0:2, 14080:14592].bitcast(F32)
    rdbC2 = SB[:, 6144:6400]

    wqv = r3(wbuf[:, 0:8192], 1024)
    wkv = r3(wbuf[:, 8192:16384], 1024)
    wvv = r3(wbuf[:, 16384:24576], 1024)
    egv = [r3(wbuf[:, bo + 0: bo + 4096], 512) for bo in (0, 12288)]
    euv = [r3(wbuf[:, bo + 4096: bo + 8192], 512) for bo in (0, 12288)]
    edv = [r3(wbuf[:, bo + 8192: bo + 12288], 1024) for bo in (0, 12288)]

    bank = [e(nc.psum_tensor(f"bank{i}", [128, 512], F32)) for i in range(8)]

    s_pe = Sem(e(nc.semaphore("s_pe")))
    s_v = Sem(e(nc.semaphore("s_v")))
    s_s = Sem(e(nc.semaphore("s_s")))
    s_p = Sem(e(nc.semaphore("s_p")))
    d_cn = Sem(e(nc.semaphore("d_cn")))
    d_wi = Sem(e(nc.semaphore("d_wi")))
    d_x = [Sem(e(nc.semaphore("d_x0"))), Sem(e(nc.semaphore("d_x1")))]
    d_xb = [Sem(e(nc.semaphore("d_xb0"))), Sem(e(nc.semaphore("d_xb1")))]
    d_xc = Sem(e(nc.semaphore("d_xc")))
    d_qt = [Sem(e(nc.semaphore("d_qt0"))), Sem(e(nc.semaphore("d_qt1")))]
    d_xq = Sem(e(nc.semaphore("d_xq")))
    d_st = Sem(e(nc.semaphore("d_st")))
    d_out = Sem(e(nc.semaphore("d_out")))
    d_wb = [Sem(e(nc.semaphore("d_wb0"))), Sem(e(nc.semaphore("d_wb1")))]
    d_we = [Sem(e(nc.semaphore("d_we0"))), Sem(e(nc.semaphore("d_we1")))]

    b = Builder()

    def V(fn, *waits):
        b.emit("vector", fn, waits=list(waits) + [(s_v, s_v.n)], inc=s_v)
        s_v.n += 1
        return s_v.n

    def S(fn, *waits):
        b.emit("scalar", fn, waits=list(waits) + [(s_s, s_s.n)], inc=s_s)
        s_s.n += 1
        return s_s.n

    def P(fn, *waits):
        b.emit("gpsimd", fn, waits=list(waits) + [(s_p, s_p.n)], inc=s_p)
        s_p.n += 1
        return s_p.n

    def M(fns, *waits):
        n = len(fns)
        for i, fn in enumerate(fns):
            b.emit("tensor", fn, waits=waits if i == 0 else (),
                   inc=s_pe if i == n - 1 else None)
        s_pe.n += 1
        return s_pe.n

    def DL(fn, sem, *waits):
        b.emit("sync", fn, waits=waits, inc=sem, k=16)
        sem.n += 16
        return sem.n

    DT = DL

    def DG(fn, sem, *waits):
        b.emit("gpsimd", fn, waits=waits, inc=sem, k=16)
        sem.n += 16
        return sem.n

    def snap():
        return (s_pe.n, s_v.n, s_s.n, s_p.n)

    def cbar():
        sn = snap()
        w = [(s_pe, sn[0]), (s_v, sn[1]), (s_s, sn[2]), (s_p, sn[3])]
        for eng, own in (("tensor", s_pe), ("vector", s_v), ("scalar", s_s), ("gpsimd", s_p)):
            wl = tuple((s.h, int(v)) for (s, v) in w if s is not own and v > 0)

            def g(e_, wl=wl):
                for h, v in wl:
                    e_.wait_ge(h, v)
            b.q[eng].append(g)
        return sn

    def mm(bk_ap, bk_id, lhsT, rhs, first, last):
        return lambda e_, o=bk_ap, l=lhsT, r=rhs, st=first, sp=last: e_.matmul(
            o, l, r, start=st, stop=sp, skip_group_check=True)

    def bank_reset(i):
        pass

    DL(lambda e_: e_.dma_start(out=onesb[:], in_=onesb_h), d_cn)
    DL(lambda e_: e_.dma_start(out=onesr[:], in_=onesr_h), d_cn)
    DL(lambda e_: e_.dma_start(out=onesrb[:], in_=onesrb_h), d_cn)
    DL(lambda e_: e_.dma_start(out=e2_s[:], in_=e2_h), d_cn)
    DL(lambda e_: e_.dma_start(out=id_s[:], in_=id_h), d_cn)
    DL(lambda e_: e_.dma_start(out=rb_s[:], in_=rb_h), d_cn)
    DL(lambda e_: e_.dma_start(out=gam_s[:], in_=gam_h), d_cn)
    DL(lambda e_: e_.dma_start(out=bet_s[:], in_=bet_h), d_cn)
    ld_const = DL(lambda e_: e_.dma_start(out=wr_s[:], in_=wr_h.rearrange("(c p) e -> p c e", p=128)), d_cn)
    DG(lambda e_: e_.dma_start(out=wqv, in_=wq_h.rearrange("(c p) o -> p c o", p=128)), d_wi)
    DG(lambda e_: e_.dma_start(out=wkv, in_=wk_h.rearrange("(c p) o -> p c o", p=128)), d_wi)
    DG(lambda e_: e_.dma_start(out=wvv, in_=wv_h.rearrange("(c p) o -> p c o", p=128)), d_wi)
    gw_init = DG(lambda e_: e_.dma_start(out=wfc_s[:], in_=wfc_h.rearrange("(c p) o -> p c o", p=128)), d_wi)

    v = P(lambda e_: e_.tensor_copy(wrhi[:], wr_s[:]), (d_cn, ld_const))
    V(lambda e_: e_.tensor_tensor(wrlo[:], wr_s[:], wrhi[:], ALU.subtract), (s_p, v))
    pe1 = M([mm(bank[6][:, 0:NE], 6, onesr[:], rb_s[:], True, True)], (d_cn, ld_const))
    V(lambda e_: e_.tensor_copy(rbb_s[:], bank[6][:, 0:NE]), (s_pe, pe1))
    bank_reset(6)
    V(lambda e_: e_.memset(vexA[:, :, 128:129], 1.0))
    V(lambda e_: e_.memset(kvbd[:], 0.0))
    V(lambda e_: e_.memset(ksbd[:], 0.0))
    vz = V(lambda e_: e_.memset(zrow[:], 0.0), (d_cn, ld_const))
    for bi in range(3):
        M([mm(bank[bi][:], bi, onesrb[:], zrow[:], True, True)], (s_v, vz))
    cbar()

    kv_banks = [(0, 0), (0, 1), (0, 2), (1, 0), (1, 1), (1, 2), (2, 0), (2, 1)]
    q_cons = {4: 0}
    A_xa = [0, 0]
    A_v2 = {}
    A_g12 = {}
    A_vf = [0, 0]
    A_vb = {0: 0}
    A_pc = 0
    A_grt1 = 0
    A_b5s = 0
    A_b5v = 0
    A_b4v = 0
    A_b6 = 0
    A_sc = 0
    ldx_pre = {}
    ldxb_pre = {}

    def emit_ldx(tt_):
        par_ = tt_ % 2
        ldx_pre[tt_] = DL(lambda e_, tt_=tt_, par_=par_: e_.dma_start(
            out=xA[par_][:],
            in_=x_h[:, tt_ * 256:(tt_ + 1) * 256].rearrange("(c p) t -> p c t", p=128)),
            d_x[par_], (s_v, A_xa[par_]))
        ldxb_pre[tt_] = DG(lambda e_, tt_=tt_, par_=par_: e_.dma_start(
            out=xb2[par_][:],
            in_=x_h[:, tt_ * 256:(tt_ + 1) * 256].rearrange("(c p) t -> p c t", p=128)),
            d_xb[par_], (s_v, A_v2.get(tt_ - 2, 0)), (s_pe, A_g12.get(tt_ - 2, (0, 0))[1]))

    def emit_front(tt_):
        par_ = tt_ % 2
        sbk = 3 if par_ == 0 else 7
        xb = xb2[par_]
        v2 = V(lambda e_, xb=xb: e_.tensor_tensor(x2A[:], xb[:], xb[:], ALU.mult),
               (d_xb[par_], ldxb_pre[tt_]), (s_pe, A_g12.get(tt_ - 1, (0, 0))[1]))
        A_v2[tt_] = v2
        g1 = M([mm(bank[sbk][0:1, 0:256], sbk, onesb[:], xb[:, c, :], c == 0, c == 7)
                for c in range(8)], (d_xb[par_], ldxb_pre[tt_]), (s_v, A_vf[par_]))
        g2 = M([mm(bank[sbk][0:1, 256:512], sbk, onesb[:], x2A[:, c, :], c == 0, c == 7)
                for c in range(8)], (s_v, v2))
        A_g12[tt_] = (g1, g2)

    def emit_stats(tt_, gb_prev_ms):
        par_ = tt_ % 2
        sbk = 3 if par_ == 0 else 7
        V(lambda e_, sbk=sbk: e_.tensor_single_scalar(slA[0], bank[sbk][0:1, 0:256],
                                                      1.0 / D, ALU.mult),
          (s_pe, max(A_g12[tt_][1], gb_prev_ms)))
        V(lambda e_, sbk=sbk: e_.tensor_single_scalar(slA[1], bank[sbk][0:1, 256:512],
                                                      1.0 / D, ALU.mult))
        V(lambda e_: e_.tensor_tensor(slA[2], slA[0], slA[0], ALU.mult))
        V(lambda e_: e_.scalar_tensor_tensor(slA[3], slA[2], -1.0, slA[1], ALU.mult, ALU.add))
        vv = V(lambda e_: e_.tensor_single_scalar(slA[1], slA[3], 1e-5, ALU.add))
        sv = S(lambda e_: e_.activation(slA[2], slA[1], AF.Sqrt), (s_v, vv))
        V(lambda e_: e_.reciprocal(slA[3], slA[2]), (s_s, sv))
        vb = V(lambda e_: e_.scalar_tensor_tensor(slA[1], slA[0], -1.0, slA[3],
                                                  ALU.mult, ALU.mult))
        A_vb[tt_] = vb

    emit_ldx(0)
    emit_ldx(1)
    emit_front(0)
    emit_stats(0, 0)

    for t in range(NT):
        par = t % 2
        sbk = 3 if par == 0 else 7
        if t + 1 < NT:
            emit_front(t + 1)

        ga = M([mm(bank[sbk][:, 0:256], sbk, onesr[:], slA[3], True, True)], (s_v, A_vb[t]))
        gb = M([mm(bank[sbk][:, 256:512], sbk, onesr[:], slA[1], True, True)])
        ab3 = bank[sbk][:, 0:256].unsqueeze(1).to_broadcast([128, 8, 256])
        bb3 = bank[sbk][:, 256:512].unsqueeze(1).to_broadcast([128, 8, 256])
        xn1m = V(lambda e_, par=par, ab3=ab3: e_.tensor_tensor(xn1A[:], xA[par][:], ab3,
                                                               ALU.mult),
                 (s_pe, gb), (s_s, A_pc), (d_x[par], ldx_pre[t]))
        A_xa[par] = xn1m
        if t + 2 < NT:
            emit_ldx(t + 2)
        vf = V(lambda e_, bb3=bb3: e_.tensor_tensor(xnfA[:], xn1A[:], bb3, ALU.add))
        A_vf[par] = vf
        pc = S(lambda e_: e_.copy(xnbA[:], xnfA[:]), (s_v, vf), (d_st, d_st.n),
               (s_pe, A_grt1))
        pc32 = S(lambda e_: e_.copy(xn1A[:], xnbA[:]))
        A_pc = pc32
        vl = V(lambda e_: e_.tensor_tensor(xnlA[:], xnfA[:], xn1A[:], ALU.subtract),
               (s_s, pc32))
        DT(lambda e_, t=t: e_.dma_start(
            out=xn_scr[:, t * 256:(t + 1) * 256].rearrange("(c p) t -> p c t", p=128),
            in_=xnbA[:]), d_st, (s_s, pc))

        sq = s_s.n
        for oc in range(8):
            w4 = [(s_v, A_b4v)] if oc == 0 else []
            gq = M([mm(bank[4][:, 0:256], 4, wqv[:, ic, oc * 128:(oc + 1) * 128],
                       xnbA[:, ic, :], ic == 0, ic == 7) for ic in range(8)],
                   (s_s, pc), (d_wi, gw_init), (s_s, q_cons[4]), *w4)
            extra = [(d_st, d_st.n)] if oc == 0 else []
            sq = S(lambda e_, oc=oc: e_.activation(qstA[:, oc, :], bank[4][:, 0:256],
                                                   AF.Sigmoid), (s_pe, gq), *extra)
            q_cons[4] = sq
        DT(lambda e_, t=t: e_.dma_start(
            out=q_scr[:, t * 256:(t + 1) * 256].rearrange("(c p) t -> p c t", p=128),
            in_=qstA[:]), d_st, (s_s, sq))

        for sti in range(2):
            def tok(ap, ic, sti=sti):
                return ap[:, ic, sti * 128:(sti + 1) * 128]
            sk_by = {}
            for fh in range(2):
                kb = 5 if fh == 0 else 4
                if fh == 0:
                    kwaits = [(s_s, A_b5s), (s_v, A_b5v)]
                else:
                    kwaits = [(s_s, q_cons[4]), (s_v, A_b4v)]
                gk = M([mm(bank[kb][:], kb, tok(xnbA, ic), wkv[:, ic, fh * 512:(fh + 1) * 512],
                           ic == 0, ic == 7) for ic in range(8)], *kwaits)
                sk_by[fh] = S(lambda e_, fh=fh, kb=kb: e_.activation(
                    ktmA[:, fh * 512:(fh + 1) * 512], bank[kb][:], AF.Sigmoid), (s_pe, gk))
            A_b5s = sk_by[0]
            q_cons[4] = sk_by[1]
            sk = sk_by[1]
            vk = 0
            for fh in range(2):
                kb = 5 if fh == 0 else 4
                gv = M([mm(bank[kb][:], kb, tok(xnbA, ic), wvv[:, ic, fh * 512:(fh + 1) * 512],
                           ic == 0, ic == 7) for ic in range(8)], (s_s, sk_by[fh]))
                vk = V(lambda e_, fh=fh, kb=kb: e_.tensor_copy(
                    vexA[:, fh * 4:(fh + 1) * 4, 0:128],
                    bank[kb][:].rearrange("p (c t) -> p c t", t=128)), (s_pe, gv))
                if fh == 0:
                    A_b5v = vk
                else:
                    A_b4v = vk
            fns = []
            for c in range(8):
                bk, slot = kv_banks[c]
                ap = bank[bk][:, slot * 129:(slot + 1) * 129]
                fns.append(mm(ap, bk, ktmA[:, c * 128:(c + 1) * 128], vexA[:, c, :],
                              False, False))
            M(fns, (s_s, sk), (s_v, vk))

            rfns = []
            first_r = True
            for ic in range(8):
                for (lh, rh) in ((xnbA, wrhi), (xnbA, wrlo), (xnlA, wrhi)):
                    rfns.append(mm(bank[6][:, sti * 32:(sti + 1) * 32], 6,
                                   tok(lh, ic), rh[:, ic, :], first_r, False))
                    first_r = False
            grt = M(rfns, (s_v, vl), (s_s, A_b6))
            if sti == 1:
                A_grt1 = grt
            ss_ = S(lambda e_, sti=sti: e_.activation(scA, bank[6][:, sti * 32:(sti + 1) * 32],
                                                      AF.Sigmoid), (s_pe, grt), (s_v, A_sc))
            A_b6 = ss_
            A_sc = V(lambda e_: e_.tensor_tensor(sfcA, scA, rbb_s[:], ALU.add), (s_s, ss_))
            sfc3 = sfcA.rearrange("p (a b) -> p a b", b=8)
            t03 = t0A.rearrange("p (a b) -> p a b", b=8)
            y3 = yA.rearrange("p (a b) -> p a b", b=8)
            m1b = m1A.unsqueeze(2).to_broadcast([128, 4, 8])
            V(lambda e_, sfc3=sfc3: e_.tensor_reduce(m1A, sfc3, AX.X, ALU.max))
            V(lambda e_, sfc3=sfc3, m1b=m1b, t03=t03: e_.tensor_tensor(t03, sfc3, m1b,
                                                                       ALU.is_equal))
            V(lambda e_, t03=t03: e_.tensor_single_scalar(t0A, t0A, NEG, ALU.mult))
            V(lambda e_, sfc3=sfc3, t03=t03, y3=y3: e_.tensor_tensor(y3, sfc3, t03, ALU.add))
            V(lambda e_, y3=y3: e_.tensor_reduce(gmkA, y3, AX.X, ALU.max))
            V(lambda e_: e_.tensor_tensor(m1A, m1A, gmkA, ALU.add))
            V(lambda e_: e_.tensor_reduce(gm1A, m1A, AX.X, ALU.max))
            V(lambda e_: e_.tensor_scalar(gmkA, m1A, gm1A, NEG, ALU.is_equal, ALU.mult))
            V(lambda e_: e_.tensor_tensor(gmkA, m1A, gmkA, ALU.add))
            V(lambda e_: e_.tensor_reduce(gm2A, gmkA, AX.X, ALU.max))
            V(lambda e_: e_.tensor_single_scalar(gmkA, m1A, gm2A, ALU.is_ge))
            gmkb = gmkA.unsqueeze(2).to_broadcast([128, 4, 8])
            tmp3 = tmpA.rearrange("p (a b) -> p a b", b=8)
            V(lambda e_, gmkb=gmkb, tmp3=tmp3, sfc3=sfc3: e_.tensor_tensor(
                tmp3, sfc3, gmkb, ALU.mult))
            V(lambda e_: e_.tensor_single_scalar(posA, tmpA, 0.0, ALU.is_gt))
            V(lambda e_: e_.tensor_reduce(pA_, posA, AX.X, ALU.add))
            V(lambda e_: e_.memset(mx8A, NEG))
            V(lambda e_: e_.tensor_copy(cA[0], tmpA))
            for j in range(4):
                src_ = cA[j % 2]
                dst = cA[(j + 1) % 2]
                V(lambda e_, src_=src_: e_.reduce_max(mx8A[:, 0:1], src_, AX.X))
                if j < 3:
                    V(lambda e_, src_=src_, dst=dst: e_.match_replace(dst, mx8A, src_, NEG))
            V(lambda e_: e_.tensor_copy(t4A, mx8A[:, 0:1]))
            V(lambda e_: e_.tensor_single_scalar(mmainA, tmpA, t4A, ALU.is_ge))
            V(lambda e_: e_.tensor_single_scalar(zA, tmpA, 0.0, ALU.is_equal))
            V(lambda e_: e_.tensor_copy(pf[0], zA))
            for k2 in range(5):
                sh = 1 << k2
                src_ = pf[k2 % 2]
                dst = pf[(k2 + 1) % 2]
                V(lambda e_, src_=src_, dst=dst, sh=sh: e_.tensor_copy(dst[:, 0:sh],
                                                                       src_[:, 0:sh]))
                V(lambda e_, src_=src_, dst=dst, sh=sh: e_.tensor_tensor(
                    dst[:, sh:NE], src_[:, sh:NE], src_[:, 0:NE - sh], ALU.add))
            pfin = pf[1]
            V(lambda e_: e_.tensor_scalar(fmpA, pA_, -1.0, 4.0, ALU.mult, ALU.add))
            V(lambda e_, pfin=pfin: e_.tensor_single_scalar(pfin, pfin, fmpA, ALU.is_le))
            V(lambda e_, pfin=pfin: e_.tensor_tensor(zA, zA, pfin, ALU.mult))
            V(lambda e_: e_.tensor_tensor(zA, zA, posA, ALU.add))
            V(lambda e_: e_.tensor_single_scalar(cndA, pA_, 4.0, ALU.is_ge))
            V(lambda e_: e_.tensor_scalar(ucndA, cndA, -1.0, 1.0, ALU.mult, ALU.add))
            V(lambda e_: e_.tensor_single_scalar(mmainA, mmainA, cndA, ALU.mult))
            V(lambda e_: e_.tensor_single_scalar(zA, zA, ucndA, ALU.mult))
            V(lambda e_: e_.tensor_tensor(mmainA, mmainA, zA, ALU.add))
            V(lambda e_: e_.tensor_tensor(wnumA, scA, mmainA, ALU.mult))
            V(lambda e_: e_.tensor_reduce(dnmA, wnumA, AX.X, ALU.add))
            V(lambda e_: e_.tensor_single_scalar(dnmA, dnmA, 1e-20, ALU.add))
            V(lambda e_: e_.reciprocal(recA, dnmA))
            V(lambda e_: e_.tensor_scalar(wtA, wnumA, recA, 2.5, ALU.mult, ALU.mult))
            V(lambda e_, t=t, sti=sti: e_.tensor_copy(w_all[:, 2 * t + sti, :], wtA))

        if t + 1 < NT:
            emit_stats(t + 1, gb)

    b7bf = bank[7][:].bitcast(BF16)
    vwe = 0
    pw_all = s_v.n
    for sub in range(2 * NT):
        gtr = M([lambda e_, sub=sub: e_.transpose(b7bf[0:32, 0:128], w_all[:, sub, :], id_s[:])],
                (s_v, max(pw_all, vwe)))
        vwe = V(lambda e_, sub=sub: e_.tensor_copy(wemA, b7bf[0:32, 0:128]),
                (s_pe, gtr), (d_st, d_st.n))
        DT(lambda e_, sub=sub: e_.dma_start(
            out=w_scr[:, sub * 128:(sub + 1) * 128], in_=wemA), d_st, (s_v, vwe))

    first_ev = [(s_pe, s_pe.n)]
    for c in range(8):
        bk, slot = kv_banks[c]
        ap = bank[bk][:, slot * 129:(slot + 1) * 129]
        V(lambda e_, ap=ap, c=c: e_.tensor_copy(kvbd[0:64, c, 0:64], ap[0:64, 0:64]), *first_ev)
        first_ev = []
        V(lambda e_, ap=ap, c=c: e_.tensor_copy(kvbd[64:128, c, 64:128], ap[64:128, 64:128]))
        V(lambda e_, ap=ap, c=c: e_.tensor_copy(ksbd[0:64, c, 0:1], ap[0:64, 128:129]))
        V(lambda e_, ap=ap, c=c: e_.tensor_copy(ksbd[64:128, c, 1:2], ap[64:128, 128:129]))
    cbar()

    gw_unit = {}
    ld_row = {}
    pe_unit_done = {}

    def emit_unit_loads(qq, uu_, same_q=True):
        gi = qq * NU + uu_
        wpar = gi % 2
        waits = []
        if gi >= 2:
            key = (qq, uu_ - 2) if uu_ >= 2 else (qq - 1, NU - 2 + uu_)
            waits.append((s_pe, pe_unit_done[key]))
        DG(lambda e_, uu_=uu_, wpar=wpar: e_.dma_start(
            out=egv[wpar], in_=gw_h[uu_].rearrange("(c p) o -> p c o", p=128)),
            d_we[wpar], *waits)
        DG(lambda e_, uu_=uu_, wpar=wpar: e_.dma_start(
            out=euv[wpar], in_=uw_h[uu_].rearrange("(c p) o -> p c o", p=128)), d_we[wpar])
        gw_unit[gi] = DG(lambda e_, uu_=uu_, wpar=wpar: e_.dma_start(
            out=edv[wpar], in_=dw_h[uu_].rearrange("(c p) o -> p c o", p=128)), d_we[wpar])
        if uu_ < NE and same_q:
            ld_row[(qq, uu_)] = DG(lambda e_, qq=qq, uu_=uu_, wpar=wpar: e_.dma_start(
                out=wbB[wpar][:],
                in_=w_scr[uu_:uu_ + 1, qq * 1024:(qq + 1) * 1024].to_broadcast([128, 1024])),
                d_wb[wpar], (d_st, d_st.n), (s_v, s_v.n), *waits)

    emit_unit_loads(0, 0)
    emit_unit_loads(0, 1)
    lagB = {"gate": {0: 0, 1: 0, 6: 0}, "up": {2: 0, 3: 0, 7: 0}, "down": {4: 0, 5: 0},
            "sg": {0: 0, 1: 0}, "ev": {0: 0, 1: 0}, "hv": {0: 0, 1: 0}}

    for q in range(NQ):
        ldxq = DL(lambda e_, q=q: e_.dma_start(
            out=xq_bf[:],
            in_=xn_scr[:, q * 1024:(q + 1) * 1024].rearrange("(c p) t -> p c t", p=128)),
            d_xq, (d_st, d_st.n), (s_v, s_v.n), (s_pe, s_pe.n))
        for u0 in (0, 1):
            if (q, u0) not in ld_row:
                ld_row[(q, u0)] = DG(lambda e_, q=q, u0=u0: e_.dma_start(
                    out=wbB[u0 % 2][:],
                    in_=w_scr[u0:u0 + 1, q * 1024:(q + 1) * 1024].to_broadcast([128, 1024])),
                    d_wb[u0 % 2], (d_st, d_st.n), (s_v, s_v.n), (s_pe, s_pe.n),
                    (s_s, s_s.n), (s_p, s_p.n))

        for u in range(NU):
            gi = q * NU + u
            wpar = gi % 2
            xq_wait = [(d_xq, ldxq)] if u == 0 else []


            for tt in range(2):
                hvp = hvB[tt]
                hv_ms = 0
                for ic in range(4):
                    gbk = [0, 1, 6][(tt * 4 + ic) % 3]
                    ubk = [2, 3, 7][(tt * 4 + ic) % 3]
                    gg = M([mm(bank[gbk][:], gbk, egv[wpar][:, kc, ic * 128:(ic + 1) * 128],
                               xq_bf[:, kc, tt * 512:(tt + 1) * 512], kc == 0, kc == 7)
                            for kc in range(8)],
                           (d_we[wpar], gw_unit[gi]), (s_s, lagB["gate"][gbk]),
                           *(xq_wait if (u == 0 and tt == 0 and ic == 0) else []))
                    sg_ = S(lambda e_, gbk=gbk, ic=ic: e_.activation(sgB[ic % 2], bank[gbk][:],
                                                                    AF.Silu),
                            (s_pe, gg), (s_v, lagB["sg"][ic % 2]))
                    lagB["gate"][gbk] = sg_
                    gu = M([mm(bank[ubk][:], ubk, euv[wpar][:, kc, ic * 128:(ic + 1) * 128],
                               xq_bf[:, kc, tt * 512:(tt + 1) * 512], kc == 0, kc == 7)
                            for kc in range(8)], (s_v, lagB["up"][ubk]))
                    war_hv = [(s_pe, lagB["hv"][tt])] if ic == 0 else []
                    if u < NE:
                        vh1 = V(lambda e_, ubk=ubk, ic=ic: e_.tensor_tensor(
                            hv1B, bank[ubk][:], sgB[ic % 2], ALU.mult), (s_pe, gu), (s_s, sg_))
                        lagB["up"][ubk] = vh1
                        lagB["sg"][ic % 2] = vh1
                        hv_ms = V(lambda e_, hvp=hvp, ic=ic, wpar=wpar, tt=tt: e_.tensor_tensor(
                            hvp[:, ic, :], hv1B, wbB[wpar][:, tt * 512:(tt + 1) * 512],
                            ALU.mult), (d_wb[wpar], ld_row[(q, u)]), *war_hv)
                    else:
                        hv_ms = V(lambda e_, hvp=hvp, ic=ic, ubk=ubk: e_.tensor_tensor(
                            hvp[:, ic, :], bank[ubk][:], sgB[ic % 2], ALU.mult),
                            (s_pe, gu), (s_s, sg_), *war_hv)
                        lagB["up"][ubk] = hv_ms
                        lagB["sg"][ic % 2] = hv_ms
                for hc in range(8):
                    dbk = 4 + hc % 2
                    dwaits = [(s_s, lagB["down"][dbk])]
                    if hc == 0:
                        dwaits.append((s_v, hv_ms))
                    gdm = M([mm(bank[dbk][:], dbk, edv[wpar][:, ic, hc * 128:(hc + 1) * 128],
                                hvp[:, ic, :], ic == 0, ic == 3) for ic in range(4)], *dwaits)
                    se_ = S(lambda e_, dbk=dbk, hc=hc: e_.activation(evB[hc % 2], bank[dbk][:],
                                                                    AF.Copy),
                            (s_pe, gdm), (s_p, lagB["ev"][hc % 2]))
                    lagB["down"][dbk] = se_
                    if u == 0:
                        ph = P(lambda e_, hc=hc, tt=tt: e_.tensor_copy(
                            mixed[:, hc, tt * 512:(tt + 1) * 512], evB[hc % 2]), (s_s, se_))
                    else:
                        ph = P(lambda e_, hc=hc, tt=tt: e_.tensor_tensor(
                            mixed[:, hc, tt * 512:(tt + 1) * 512], evB[hc % 2],
                            mixed[:, hc, tt * 512:(tt + 1) * 512], ALU.add), (s_s, se_))
                    lagB["ev"][hc % 2] = ph
                lagB["hv"][tt] = s_pe.n
            pe_unit_done[(q, u)] = s_pe.n
            nq2, nu2 = (q, u + 2) if u + 2 < NU else (q + 1, u + 2 - NU)
            if nq2 < NQ:
                emit_unit_loads(nq2, nu2, same_q=(nq2 == q))

        spool = s_p.n
        for hc in range(8):
            S(lambda e_, hc=hc: e_.activation(gq_bf[:, hc, :], mixed[:, hc, :], AF.Silu),
              (s_p, spool))
        cbar()

        lagC = {"num": {0: 0, 1: 0}, "den": {4: 0, 6: 0}, "rd": {5: 0, 7: 0},
                "fc": {2: 0, 3: 0}, "pv2": 0, "h2": 0}
        qt_free = {0: 0, 1: 0}
        ldq_pre = {}
        ldq_pre[0] = DL(lambda e_, q=q: e_.dma_start(
            out=qtC[0][:],
            in_=q_scr[:, (q * 4) * 256:(q * 4 + 1) * 256].rearrange("(c p) t -> p c t", p=128)),
            d_qt[0], (d_st, d_st.n), (s_v, s_v.n), (s_pe, s_pe.n), (s_s, s_s.n), (s_p, s_p.n))
        ldq_pre[1] = DL(lambda e_, q=q: e_.dma_start(
            out=qtC[1][:],
            in_=q_scr[:, (q * 4 + 1) * 256:(q * 4 + 2) * 256].rearrange("(c p) t -> p c t", p=128)),
            d_qt[1], (d_st, d_st.n), (s_v, s_v.n), (s_pe, s_pe.n), (s_s, s_s.n), (s_p, s_p.n))
        for ct in range(4):
            gt = q * 4 + ct
            par = ct % 2
            ldq_ = ldq_pre[ct]
            ldxc = DL(lambda e_, gt=gt: e_.dma_start(
                out=xC[:],
                in_=x_h[:, gt * 256:(gt + 1) * 256].rearrange("(c p) t -> p c t", p=128)),
                d_xc, (s_v, s_v.n), (s_s, s_s.n), (s_p, s_p.n))
            for c in range(8):
                nbk = c % 2
                dnb = 4 + 2 * (c % 2)
                rdb = 5 + 2 * (c % 2)
                gn = M([mm(bank[nbk][:, 0:256], nbk, kvbd[:, c, :], qtC[par][:, c, :],
                           True, True)], (d_qt[par], ldq_), (s_v, lagC["num"][nbk]))
                gdn = M([mm(bank[dnb][0:2, 0:256], dnb, ksbd[:, c, :], qtC[par][:, c, :],
                            True, True)], (s_v, lagC["den"][dnb]))
                vr0 = V(lambda e_, dnb=dnb: e_.tensor_single_scalar(
                    rdC, bank[dnb][0:2, 0:256], 1e-8, ALU.add), (s_pe, gdn))
                lagC["den"][dnb] = vr0
                vr = V(lambda e_: e_.reciprocal(rd2C, rdC))
                gb_ = M([mm(bank[rdb][:, 0:256], rdb, e2_s[:], rd2C, True, True)],
                        (s_v, max(vr, lagC["rd"][rdb])))
                vc1 = V(lambda e_, rdb=rdb: e_.tensor_copy(rdbC, bank[rdb][:, 0:256]), (s_pe, gb_))
                lagC["rd"][rdb] = vc1
                nd_last = V(lambda e_, c=c, nbk=nbk: e_.tensor_tensor(
                    ndC[:, c, :], bank[nbk][:, 0:256], rdbC, ALU.mult), (s_pe, gn))
                lagC["num"][nbk] = nd_last
            qt_free[par] = s_pe.n
            if ct + 2 < 4:
                ldq_pre[ct + 2] = DL(lambda e_, gt=gt, par=par: e_.dma_start(
                    out=qtC[par][:],
                    in_=q_scr[:, (gt + 2) * 256:(gt + 3) * 256].rearrange("(c p) t -> p c t", p=128)),
                    d_qt[par], (s_pe, qt_free[par]))
            for oc in range(8):
                fbk = 2 + oc % 2
                gf = M([mm(bank[fbk][:, 0:256], fbk, wfc_s[:, ic, oc * 128:(oc + 1) * 128],
                           ndC[:, ic, :], ic == 0, ic == 7) for ic in range(8)],
                       (s_v, max(lagC["fc"][fbk], nd_last)))
                exf = [(s_p, lagC["pv2"])] if oc == 0 else []
                lagC["fc"][fbk] = V(lambda e_, oc=oc, fbk=fbk, ct=ct: e_.tensor_tensor(
                    xatC[:, oc, :], bank[fbk][:, 0:256],
                    xq_bf[:, oc, ct * 256:(ct + 1) * 256], ALU.add), (s_pe, gf), *exf)
            pv2 = P(lambda e_: e_.tensor_copy(xabC[:], xatC[:]), (s_v, s_v.n),
                    (s_pe, lagC["h2"]))
            lagC["pv2"] = pv2
            v2b = V(lambda e_: e_.tensor_tensor(xsqC[:], xabC[:], xabC[:], ALU.mult), (s_p, pv2))
            bank_reset(6)
            h1 = M([mm(bank[6][0:1, 0:256], 6, onesb[:], xabC[:, c, :], c == 0, c == 7)
                    for c in range(8)], (s_p, pv2), (s_v, lagC["den"][6]))
            h2 = M([mm(bank[6][0:1, 256:512], 6, onesb[:], xsqC[:, c, :], c == 0, c == 7)
                    for c in range(8)], (s_v, v2b))
            lagC["h2"] = h2
            V(lambda e_: e_.tensor_single_scalar(slC[0], bank[6][0:1, 0:256], 1.0 / D, ALU.mult),
              (s_pe, h2))
            V(lambda e_: e_.tensor_single_scalar(slC[1], bank[6][0:1, 256:512], 1.0 / D, ALU.mult))
            V(lambda e_: e_.tensor_tensor(slC[2], slC[0], slC[0], ALU.mult))
            V(lambda e_: e_.scalar_tensor_tensor(slC[3], slC[2], -1.0, slC[1],
                                                 ALU.mult, ALU.add))
            vv2 = V(lambda e_: e_.tensor_single_scalar(slC[1], slC[3], 1e-5, ALU.add))
            sv2 = S(lambda e_: e_.activation(slC[2], slC[1], AF.Sqrt), (s_v, vv2))
            V(lambda e_: e_.reciprocal(slC[3], slC[2]), (s_s, sv2))
            vb2 = V(lambda e_: e_.scalar_tensor_tensor(slC[1], slC[0], -1.0, slC[3],
                                                       ALU.mult, ALU.mult))
            bank_reset(6)
            ga2 = M([mm(bank[6][:, 0:256], 6, onesr[:], slC[3], True, True)], (s_v, vb2))
            gb2 = M([mm(bank[6][:, 256:512], 6, onesr[:], slC[1], True, True)])
            ab3 = bank[6][:, 0:256].unsqueeze(1).to_broadcast([128, 8, 256])
            bb3 = bank[6][:, 256:512].unsqueeze(1).to_broadcast([128, 8, 256])
            V(lambda e_, ab3=ab3: e_.tensor_tensor(t1C[:], xatC[:], ab3, ALU.mult), (s_pe, gb2))
            lagC["den"][6] = V(lambda e_, bb3=bb3: e_.tensor_tensor(t2C[:], t1C[:], bb3,
                                                                    ALU.add))
            for oc in range(8):
                V(lambda e_, oc=oc: e_.tensor_scalar(sC[:, oc, :], t2C[:, oc, :],
                                                     gam_s[:, oc:oc + 1], bet_s[:, oc:oc + 1],
                                                     ALU.mult, ALU.add))
            V(lambda e_, ct=ct: e_.tensor_tensor(
                p1C[:], sC[:], gq_bf[:, :, ct * 256:(ct + 1) * 256], ALU.mult))
            vo_ = V(lambda e_: e_.tensor_tensor(outC[:], p1C[:], xC[:], ALU.add),
                    (d_xc, ldxc), (d_out, d_out.n))
            DT(lambda e_, gt=gt: e_.dma_start(
                out=out_h[:, gt * 256:(gt + 1) * 256].rearrange("(c p) t -> p c t", p=128),
                in_=outC[:]), d_out, (s_v, vo_))
            if ct == 3:
                cbar()

    all_dma = [d_out, d_cn, d_wi, d_xc, d_qt[0], d_qt[1], d_xq, d_st,
               d_wb[0], d_wb[1], d_we[0], d_we[1], d_x[0], d_x[1], d_xb[0], d_xb[1]]

    def g_final(e_, finals=tuple((s.h, s.n) for s in all_dma)):
        for h, v in finals:
            e_.wait_ge(h, v)
    b.q["sync"].append(g_final)

    blk = e(nc.Block())
    for eng_name in ("sync", "gpsimd", "tensor", "scalar", "vector"):
        ops = b.q[eng_name]

        def body(engine, ops=ops):
            for f in ops:
                f(engine)
        getattr(blk, eng_name)(body)
    stack.close()
    return nc


def _prep(inputs):
    x = np.asarray(inputs["L0_attn_inputs"], dtype=np.float32)
    B = x.shape[0]
    x_fm = np.ascontiguousarray(x.transpose(0, 2, 1))
    f32 = lambda a: np.ascontiguousarray(np.asarray(a, dtype=np.float32))
    gw_t = np.asarray(inputs["L0_ffn_gate_w"], dtype=np.float32).transpose(0, 2, 1)
    uw_t = np.asarray(inputs["L0_ffn_up_w"], dtype=np.float32).transpose(0, 2, 1)
    dw_t = np.asarray(inputs["L0_ffn_down_w"], dtype=np.float32).transpose(0, 2, 1)
    sg_t = np.asarray(inputs["L0_ffn_shared_gate_w"], dtype=np.float32).T
    su_t = np.asarray(inputs["L0_ffn_shared_up_w"], dtype=np.float32).T
    sd_t = np.asarray(inputs["L0_ffn_shared_down_w"], dtype=np.float32).T
    sg2 = sg_t.reshape(D, 2, I_HALF).transpose(1, 0, 2)
    su2 = su_t.reshape(D, 2, I_HALF).transpose(1, 0, 2)
    sd2 = sd_t.reshape(2, I_HALF, D)
    e2 = np.zeros((2, 128), dtype=np.float32)
    e2[0, 0:64] = 1.0
    e2[1, 64:128] = 1.0
    shared = {
        "wq": f32(inputs["L0_attn_W_Q"]),
        "wk": f32(inputs["L0_attn_W_K"]),
        "wv": f32(inputs["L0_attn_W_V"]),
        "wfc": f32(inputs["L0_attn_W_fc"]),
        "wr_t": f32(np.asarray(inputs["L0_ffn_router_weight"], dtype=np.float32).T),
        "gamma2": f32(np.asarray(inputs["L0_attn_gamma"], dtype=np.float32).reshape(C8, 128).T),
        "beta2": f32(np.asarray(inputs["L0_attn_beta"], dtype=np.float32).reshape(C8, 128).T),
        "rbias": f32(np.asarray(inputs["L0_ffn_router_bias"], dtype=np.float32).reshape(1, NE)),
        "gw_all": f32(np.concatenate([gw_t, sg2], axis=0)),
        "uw_all": f32(np.concatenate([uw_t, su2], axis=0)),
        "dw_all": f32(np.concatenate([dw_t, sd2], axis=0)),
        "ones_col": np.ones((128, 1), dtype=ml_dtypes.bfloat16),
        "ones_row": np.ones((1, 128), dtype=np.float32),
        "ones_row_bf": np.ones((1, 128), dtype=ml_dtypes.bfloat16),
        "id128": np.eye(128, dtype=ml_dtypes.bfloat16),
        "e2": e2,
    }
    in_maps = []
    for bi in range(B):
        m = dict(shared)
        m["x_fm"] = x_fm[bi]
        in_maps.append(m)
    return in_maps


def kernel(**inputs):
    in_maps = _prep(inputs)
    T = in_maps[0]["x_fm"].shape[1]
    nc = build_core(T)
    res = run_bass_kernel_spmd(nc, in_maps, core_ids=list(range(len(in_maps))))
    outs = [np.asarray(r["out_fm"], dtype=np.float32).T for r in res.results]
    return np.ascontiguousarray(np.stack(outs, axis=0))


# version 12
# speedup vs baseline: 1.0750x; 1.0750x over previous
import numpy as np
import ml_dtypes
import concourse.bass as bass
import concourse.mybir as mybir
from concourse.bass_utils import run_bass_kernel_spmd

F32 = mybir.dt.float32
BF16 = mybir.dt.bfloat16
AF = mybir.ActivationFunctionType
ALU = mybir.AluOpType
AX = mybir.AxisListType

D = 1024
C8 = 8
NE = 32
NU = 34
I_HALF = 512
NEG = -1.0e30


class Sem:
    def __init__(self, h):
        self.h = h
        self.n = 0


class Builder:
    def __init__(self):
        self.q = {k: [] for k in ("tensor", "vector", "scalar", "gpsimd", "sync")}

    def emit(self, eng, fn, waits=(), inc=None, k=1):
        wl = tuple((s.h, int(v)) for (s, v) in waits if v > 0)
        ih = inc.h if inc is not None else None

        def g(e, wl=wl, fn=fn, ih=ih, k=k):
            for h, v in wl:
                e.wait_ge(h, v)
            r = fn(e)
            if ih is not None:
                r.then_inc(ih, k)
        self.q[eng].append(g)


def build_core(T, dbg=False):
    NT = T // 256
    NQ = T // 1024
    nc = bass.Bass("TRN2", target_bir_lowering=False)

    io = "ExternalOutput" if dbg else "Internal"
    x_h = nc.dram_tensor("x_fm", [D, T], F32, kind="ExternalInput").ap()
    wq_h = nc.dram_tensor("wq", [D, D], F32, kind="ExternalInput").ap()
    wk_h = nc.dram_tensor("wk", [D, D], F32, kind="ExternalInput").ap()
    wv_h = nc.dram_tensor("wv", [D, D], F32, kind="ExternalInput").ap()
    wfc_h = nc.dram_tensor("wfc", [D, D], F32, kind="ExternalInput").ap()
    wr_h = nc.dram_tensor("wr_t", [D, NE], F32, kind="ExternalInput").ap()
    gam_h = nc.dram_tensor("gamma2", [128, C8], F32, kind="ExternalInput").ap()
    bet_h = nc.dram_tensor("beta2", [128, C8], F32, kind="ExternalInput").ap()
    rb_h = nc.dram_tensor("rbias", [1, NE], F32, kind="ExternalInput").ap()
    gw_h = nc.dram_tensor("gw_all", [NU, D, I_HALF], F32, kind="ExternalInput").ap()
    uw_h = nc.dram_tensor("uw_all", [NU, D, I_HALF], F32, kind="ExternalInput").ap()
    dw_h = nc.dram_tensor("dw_all", [NU, I_HALF, D], F32, kind="ExternalInput").ap()
    onesb_h = nc.dram_tensor("ones_col", [128, 1], BF16, kind="ExternalInput").ap()
    onesr_h = nc.dram_tensor("ones_row", [1, 128], F32, kind="ExternalInput").ap()
    onesrb_h = nc.dram_tensor("ones_row_bf", [1, 128], BF16, kind="ExternalInput").ap()
    e2_h = nc.dram_tensor("e2", [2, 128], F32, kind="ExternalInput").ap()
    id_h = nc.dram_tensor("id128", [128, 128], BF16, kind="ExternalInput").ap()

    out_h = nc.dram_tensor("out_fm", [D, T], F32, kind="ExternalOutput").ap()
    xn_scr = nc.dram_tensor("xn_scr", [D, T], BF16, kind=io).ap()
    q_scr = nc.dram_tensor("q_scr", [D, T], BF16, kind=io).ap()
    w_scr = nc.dram_tensor("w_scr", [NE, T], BF16, kind=io).ap()

    from contextlib import ExitStack
    stack = ExitStack()
    e = stack.enter_context

    wbuf = e(nc.sbuf_tensor("wbuf", [128, 24576], BF16))
    wfc_s = e(nc.sbuf_tensor("wfc_s", [128, C8, 1024], BF16))
    mixed = e(nc.sbuf_tensor("mixed", [128, C8, 1024], F32))
    xq_bf = e(nc.sbuf_tensor("xq_bf", [128, C8, 1024], BF16))
    gq_bf = e(nc.sbuf_tensor("gq_bf", [128, C8, 1024], BF16))
    kvbd = e(nc.sbuf_tensor("kvbd", [128, C8, 128], BF16))
    ksbd = e(nc.sbuf_tensor("ksbd", [128, C8, 2], BF16))
    gam_s = e(nc.sbuf_tensor("gam_s", [128, C8], F32))
    bet_s = e(nc.sbuf_tensor("bet_s", [128, C8], F32))
    onesb = e(nc.sbuf_tensor("onesb", [128, 1], BF16))
    onesr = e(nc.sbuf_tensor("onesr", [1, 128], F32))
    onesrb = e(nc.sbuf_tensor("onesrb", [1, 128], BF16))
    e2_s = e(nc.sbuf_tensor("e2_s", [2, 128], F32))
    id_s = e(nc.sbuf_tensor("id_s", [128, 128], BF16))
    rb_s = e(nc.sbuf_tensor("rb_s", [1, NE], F32))
    rbb_s = e(nc.sbuf_tensor("rbb_s", [128, NE], F32))
    wr_s = e(nc.sbuf_tensor("wr_s", [128, C8, NE], F32))
    wrhi = e(nc.sbuf_tensor("wrhi", [128, C8, NE], BF16))
    wrlo = e(nc.sbuf_tensor("wrlo", [128, C8, NE], BF16))
    zrow = e(nc.sbuf_tensor("zrow", [1, 512], BF16))

    SF = e(nc.sbuf_tensor("scr_f32", [128, 10240], F32))
    SB = e(nc.sbuf_tensor("scr_bf", [128, 15616], BF16))

    r3 = lambda ap, t: ap.rearrange("p (c t) -> p c t", t=t)
    xA = [r3(SF[:, i * 2048:(i + 1) * 2048], 256) for i in range(2)]
    xn1A = r3(SF[:, 4096:6144], 256)
    xnfA = r3(SF[:, 6144:8192], 256)
    scA = SF[:, 8192:8224]
    sfcA = SF[:, 8224:8256]
    m1A = SF[:, 8256:8260]
    gmkA = SF[:, 8260:8264]
    gm1A = SF[:, 8264:8265]
    gm2A = SF[:, 8265:8266]
    t0A = SF[:, 8266:8298]
    yA = SF[:, 8298:8330]
    tmpA = SF[:, 8330:8362]
    posA = SF[:, 8362:8394]
    pA_ = SF[:, 8394:8395]
    mx8A = SF[:, 8395:8403]
    t4A = SF[:, 8403:8404]
    fmpA = SF[:, 8404:8405]
    cndA = SF[:, 8405:8406]
    ucndA = SF[:, 8406:8407]
    dnmA = SF[:, 8407:8408]
    recA = SF[:, 8408:8409]
    cA = [SF[:, 8409 + i * 32: 8441 + i * 32] for i in range(2)]
    mmainA = SF[:, 8473:8505]
    zA = SF[:, 8505:8537]
    pf = [SF[:, 8537 + i * 32: 8569 + i * 32] for i in range(2)]
    wnumA = SF[:, 8601:8633]
    wtA = SF[:, 8633:8665]
    slA = [SF[0:1, 8960 + i * 256: 9216 + i * 256] for i in range(4)]

    xb2 = [r3(SB[:, 0:2048], 256), r3(SB[:, 13568:15616], 256)]
    xbA = xb2[0]
    x2A = r3(SB[:, 2048:4096], 256)
    xnbA = r3(SB[:, 4096:6144], 256)
    xnlA = r3(SB[:, 6144:8192], 256)
    qstA = r3(SB[:, 8192:10240], 256)
    ktmA = SB[:, 10240:11264]
    vexA = r3(SB[:, 11264:12296], 129)
    wtbA = SB[:, 12296:12328]
    w_all = r3(SB[:, 12544:13568], 32)
    wemA = SB[0:32, 12328:12456]

    sgB = [SB[:, i * 512:(i + 1) * 512] for i in range(2)]
    hv1B = SB[:, 1024:1536]
    hvB = [r3(SB[:, 1536 + i * 2048: 3584 + i * 2048], 512) for i in range(2)]
    wbB = [SB[:, 5632 + i * 1024: 6656 + i * 1024] for i in range(2)]
    evB = [SF[:, i * 512:(i + 1) * 512] for i in range(2)]

    xC = r3(SF[:, 0:2048], 256)
    xatC = r3(SF[:, 2048:4096], 256)
    t1C = r3(SF[:, 4096:6144], 256)
    t2C = r3(SF[:, 6144:8192], 256)
    p1C = t1C
    outC = r3(SF[:, 8192:10240], 256)
    qtC = [r3(SB[:, i * 2048:(i + 1) * 2048], 256) for i in range(2)]
    ndC = r3(SB[:, 4096:6144], 256)
    xabC = r3(SB[:, 6144:8192], 256)
    xsqC = r3(SB[:, 8192:10240], 256)
    sC = r3(SB[:, 10240:12288], 256)
    rdbC = SB[:, 12288:12544]
    slC = [SB[0:1, 12544 + i * 512: 13056 + i * 512].bitcast(F32) for i in range(4)]
    rdC = SB[0:2, 12544:13056].bitcast(F32)
    rd2C = SB[0:2, 13056:13568].bitcast(F32)
    rdC2b = SB[0:2, 13568:14080].bitcast(F32)
    rd2C2b = SB[0:2, 14080:14592].bitcast(F32)
    rdbC2 = SB[:, 6144:6400]

    wqv = r3(wbuf[:, 0:8192], 1024)
    wkv = r3(wbuf[:, 8192:16384], 1024)
    wvv = r3(wbuf[:, 16384:24576], 1024)
    egv = [r3(wbuf[:, bo + 0: bo + 4096], 512) for bo in (0, 12288)]
    euv = [r3(wbuf[:, bo + 4096: bo + 8192], 512) for bo in (0, 12288)]
    edv = [r3(wbuf[:, bo + 8192: bo + 12288], 1024) for bo in (0, 12288)]

    bank = [e(nc.psum_tensor(f"bank{i}", [128, 512], F32)) for i in range(8)]

    s_pe = Sem(e(nc.semaphore("s_pe")))
    s_v = Sem(e(nc.semaphore("s_v")))
    s_s = Sem(e(nc.semaphore("s_s")))
    s_p = Sem(e(nc.semaphore("s_p")))
    d_cn = Sem(e(nc.semaphore("d_cn")))
    d_wi = Sem(e(nc.semaphore("d_wi")))
    d_x = [Sem(e(nc.semaphore("d_x0"))), Sem(e(nc.semaphore("d_x1")))]
    d_xb = [Sem(e(nc.semaphore("d_xb0"))), Sem(e(nc.semaphore("d_xb1")))]
    d_xc = Sem(e(nc.semaphore("d_xc")))
    d_qt = [Sem(e(nc.semaphore("d_qt0"))), Sem(e(nc.semaphore("d_qt1")))]
    d_xq = Sem(e(nc.semaphore("d_xq")))
    d_st = Sem(e(nc.semaphore("d_st")))
    d_out = Sem(e(nc.semaphore("d_out")))
    d_wb = [Sem(e(nc.semaphore("d_wb0"))), Sem(e(nc.semaphore("d_wb1")))]
    d_we = [Sem(e(nc.semaphore("d_we0"))), Sem(e(nc.semaphore("d_we1")))]

    b = Builder()

    def V(fn, *waits):
        b.emit("vector", fn, waits=list(waits) + [(s_v, s_v.n)], inc=s_v)
        s_v.n += 1
        return s_v.n

    def S(fn, *waits):
        b.emit("scalar", fn, waits=list(waits) + [(s_s, s_s.n)], inc=s_s)
        s_s.n += 1
        return s_s.n

    def P(fn, *waits):
        b.emit("gpsimd", fn, waits=list(waits) + [(s_p, s_p.n)], inc=s_p)
        s_p.n += 1
        return s_p.n

    def M(fns, *waits):
        n = len(fns)
        for i, fn in enumerate(fns):
            b.emit("tensor", fn, waits=waits if i == 0 else (),
                   inc=s_pe if i == n - 1 else None)
        s_pe.n += 1
        return s_pe.n

    def DL(fn, sem, *waits):
        b.emit("sync", fn, waits=waits, inc=sem, k=16)
        sem.n += 16
        return sem.n

    DT = DL

    def DG(fn, sem, *waits):
        b.emit("gpsimd", fn, waits=waits, inc=sem, k=16)
        sem.n += 16
        return sem.n

    def snap():
        return (s_pe.n, s_v.n, s_s.n, s_p.n)

    def cbar():
        sn = snap()
        w = [(s_pe, sn[0]), (s_v, sn[1]), (s_s, sn[2]), (s_p, sn[3])]
        for eng, own in (("tensor", s_pe), ("vector", s_v), ("scalar", s_s), ("gpsimd", s_p)):
            wl = tuple((s.h, int(v)) for (s, v) in w if s is not own and v > 0)

            def g(e_, wl=wl):
                for h, v in wl:
                    e_.wait_ge(h, v)
            b.q[eng].append(g)
        return sn

    def mm(bk_ap, bk_id, lhsT, rhs, first, last):
        return lambda e_, o=bk_ap, l=lhsT, r=rhs, st=first, sp=last: e_.matmul(
            o, l, r, start=st, stop=sp, skip_group_check=True)

    def bank_reset(i):
        pass

    DL(lambda e_: e_.dma_start(out=onesb[:], in_=onesb_h), d_cn)
    DL(lambda e_: e_.dma_start(out=onesr[:], in_=onesr_h), d_cn)
    DL(lambda e_: e_.dma_start(out=onesrb[:], in_=onesrb_h), d_cn)
    DL(lambda e_: e_.dma_start(out=e2_s[:], in_=e2_h), d_cn)
    DL(lambda e_: e_.dma_start(out=id_s[:], in_=id_h), d_cn)
    DL(lambda e_: e_.dma_start(out=rb_s[:], in_=rb_h), d_cn)
    DL(lambda e_: e_.dma_start(out=gam_s[:], in_=gam_h), d_cn)
    DL(lambda e_: e_.dma_start(out=bet_s[:], in_=bet_h), d_cn)
    ld_const = DL(lambda e_: e_.dma_start(out=wr_s[:], in_=wr_h.rearrange("(c p) e -> p c e", p=128)), d_cn)
    DG(lambda e_: e_.dma_start(out=wqv, in_=wq_h.rearrange("(c p) o -> p c o", p=128)), d_wi)
    DG(lambda e_: e_.dma_start(out=wkv, in_=wk_h.rearrange("(c p) o -> p c o", p=128)), d_wi)
    DG(lambda e_: e_.dma_start(out=wvv, in_=wv_h.rearrange("(c p) o -> p c o", p=128)), d_wi)
    gw_init = DG(lambda e_: e_.dma_start(out=wfc_s[:], in_=wfc_h.rearrange("(c p) o -> p c o", p=128)), d_wi)

    v = P(lambda e_: e_.tensor_copy(wrhi[:], wr_s[:]), (d_cn, ld_const))
    V(lambda e_: e_.tensor_tensor(wrlo[:], wr_s[:], wrhi[:], ALU.subtract), (s_p, v))
    pe1 = M([mm(bank[6][:, 0:NE], 6, onesr[:], rb_s[:], True, True)], (d_cn, ld_const))
    V(lambda e_: e_.tensor_copy(rbb_s[:], bank[6][:, 0:NE]), (s_pe, pe1))
    bank_reset(6)
    V(lambda e_: e_.memset(vexA[:, :, 128:129], 1.0))
    V(lambda e_: e_.memset(kvbd[:], 0.0))
    V(lambda e_: e_.memset(ksbd[:], 0.0))
    vz = V(lambda e_: e_.memset(zrow[:], 0.0), (d_cn, ld_const))
    for bi in range(3):
        M([mm(bank[bi][:], bi, onesrb[:], zrow[:], True, True)], (s_v, vz))
    cbar()

    kv_banks = [(0, 0), (0, 1), (0, 2), (1, 0), (1, 1), (1, 2), (2, 0), (2, 1)]
    q_cons = {4: 0}
    A_xa = [0, 0]
    A_v2 = {}
    A_g12 = {}
    A_vf = [0, 0]
    A_vb = {0: 0}
    A_pc = 0
    A_grt1 = 0
    A_b5s = 0
    A_b5v = 0
    A_b4v = 0
    A_b6 = 0
    A_sc = 0
    ldx_pre = {}
    ldxb_pre = {}

    def emit_ldx(tt_):
        par_ = tt_ % 2
        ldx_pre[tt_] = DL(lambda e_, tt_=tt_, par_=par_: e_.dma_start(
            out=xA[par_][:],
            in_=x_h[:, tt_ * 256:(tt_ + 1) * 256].rearrange("(c p) t -> p c t", p=128)),
            d_x[par_], (s_v, A_xa[par_]))

    def emit_front(tt_):
        par_ = tt_ % 2
        sbk = 3 if par_ == 0 else 7
        xb = xb2[par_]
        sc_ = S(lambda e_, xb=xb, par_=par_: e_.copy(xb[:], xA[par_][:]),
                (d_x[par_], ldx_pre[tt_]), (s_v, A_v2.get(tt_ - 2, 0)),
                (s_pe, A_g12.get(tt_ - 2, (0, 0))[1]))
        v2 = V(lambda e_, xb=xb: e_.tensor_tensor(x2A[:], xb[:], xb[:], ALU.mult),
               (s_s, sc_), (s_pe, A_g12.get(tt_ - 1, (0, 0))[1]))
        A_v2[tt_] = v2
        g1 = M([mm(bank[sbk][0:1, 0:256], sbk, onesb[:], xb[:, c, :], c == 0, c == 7)
                for c in range(8)], (s_s, sc_), (s_v, A_vf[par_]))
        g2 = M([mm(bank[sbk][0:1, 256:512], sbk, onesb[:], x2A[:, c, :], c == 0, c == 7)
                for c in range(8)], (s_v, v2))
        A_g12[tt_] = (g1, g2)

    def emit_stats(tt_, gb_prev_ms):
        par_ = tt_ % 2
        sbk = 3 if par_ == 0 else 7
        V(lambda e_, sbk=sbk: e_.tensor_single_scalar(slA[0], bank[sbk][0:1, 0:256],
                                                      1.0 / D, ALU.mult),
          (s_pe, max(A_g12[tt_][1], gb_prev_ms)))
        V(lambda e_, sbk=sbk: e_.tensor_single_scalar(slA[1], bank[sbk][0:1, 256:512],
                                                      1.0 / D, ALU.mult))
        V(lambda e_: e_.tensor_tensor(slA[2], slA[0], slA[0], ALU.mult))
        V(lambda e_: e_.scalar_tensor_tensor(slA[3], slA[2], -1.0, slA[1], ALU.mult, ALU.add))
        vv = V(lambda e_: e_.tensor_single_scalar(slA[1], slA[3], 1e-5, ALU.add))
        sv = S(lambda e_: e_.activation(slA[2], slA[1], AF.Sqrt), (s_v, vv))
        V(lambda e_: e_.reciprocal(slA[3], slA[2]), (s_s, sv))
        vb = V(lambda e_: e_.scalar_tensor_tensor(slA[1], slA[0], -1.0, slA[3],
                                                  ALU.mult, ALU.mult))
        A_vb[tt_] = vb

    emit_ldx(0)
    emit_ldx(1)
    emit_front(0)
    emit_stats(0, 0)

    for t in range(NT):
        par = t % 2
        sbk = 3 if par == 0 else 7
        if t + 1 < NT:
            emit_front(t + 1)

        ga = M([mm(bank[sbk][:, 0:256], sbk, onesr[:], slA[3], True, True)], (s_v, A_vb[t]))
        gb = M([mm(bank[sbk][:, 256:512], sbk, onesr[:], slA[1], True, True)])
        ab3 = bank[sbk][:, 0:256].unsqueeze(1).to_broadcast([128, 8, 256])
        bb3 = bank[sbk][:, 256:512].unsqueeze(1).to_broadcast([128, 8, 256])
        xn1m = V(lambda e_, par=par, ab3=ab3: e_.tensor_tensor(xn1A[:], xA[par][:], ab3,
                                                               ALU.mult),
                 (s_pe, gb), (s_s, A_pc), (d_x[par], ldx_pre[t]))
        A_xa[par] = xn1m
        if t + 2 < NT:
            emit_ldx(t + 2)
        vf = V(lambda e_, bb3=bb3: e_.tensor_tensor(xnfA[:], xn1A[:], bb3, ALU.add))
        A_vf[par] = vf
        pc = S(lambda e_: e_.copy(xnbA[:], xnfA[:]), (s_v, vf), (d_st, d_st.n),
               (s_pe, A_grt1))
        pc32 = S(lambda e_: e_.copy(xn1A[:], xnbA[:]))
        A_pc = pc32
        vl = V(lambda e_: e_.tensor_tensor(xnlA[:], xnfA[:], xn1A[:], ALU.subtract),
               (s_s, pc32))
        DT(lambda e_, t=t: e_.dma_start(
            out=xn_scr[:, t * 256:(t + 1) * 256].rearrange("(c p) t -> p c t", p=128),
            in_=xnbA[:]), d_st, (s_s, pc))

        sq = s_s.n
        for oc in range(8):
            w4 = [(s_v, A_b4v)] if oc == 0 else []
            gq = M([mm(bank[4][:, 0:256], 4, wqv[:, ic, oc * 128:(oc + 1) * 128],
                       xnbA[:, ic, :], ic == 0, ic == 7) for ic in range(8)],
                   (s_s, pc), (d_wi, gw_init), (s_s, q_cons[4]), *w4)
            extra = [(d_st, d_st.n)] if oc == 0 else []
            sq = S(lambda e_, oc=oc: e_.activation(qstA[:, oc, :], bank[4][:, 0:256],
                                                   AF.Sigmoid), (s_pe, gq), *extra)
            q_cons[4] = sq
        DT(lambda e_, t=t: e_.dma_start(
            out=q_scr[:, t * 256:(t + 1) * 256].rearrange("(c p) t -> p c t", p=128),
            in_=qstA[:]), d_st, (s_s, sq))

        for sti in range(2):
            def tok(ap, ic, sti=sti):
                return ap[:, ic, sti * 128:(sti + 1) * 128]
            sk_by = {}
            for fh in range(2):
                kb = 5 if fh == 0 else 4
                if fh == 0:
                    kwaits = [(s_s, A_b5s), (s_v, A_b5v)]
                else:
                    kwaits = [(s_s, q_cons[4]), (s_v, A_b4v)]
                gk = M([mm(bank[kb][:], kb, tok(xnbA, ic), wkv[:, ic, fh * 512:(fh + 1) * 512],
                           ic == 0, ic == 7) for ic in range(8)], *kwaits)
                sk_by[fh] = S(lambda e_, fh=fh, kb=kb: e_.activation(
                    ktmA[:, fh * 512:(fh + 1) * 512], bank[kb][:], AF.Sigmoid), (s_pe, gk))
            A_b5s = sk_by[0]
            q_cons[4] = sk_by[1]
            sk = sk_by[1]
            vk = 0
            for fh in range(2):
                kb = 5 if fh == 0 else 4
                gv = M([mm(bank[kb][:], kb, tok(xnbA, ic), wvv[:, ic, fh * 512:(fh + 1) * 512],
                           ic == 0, ic == 7) for ic in range(8)], (s_s, sk_by[fh]))
                vk = V(lambda e_, fh=fh, kb=kb: e_.tensor_copy(
                    vexA[:, fh * 4:(fh + 1) * 4, 0:128],
                    bank[kb][:].rearrange("p (c t) -> p c t", t=128)), (s_pe, gv))
                if fh == 0:
                    A_b5v = vk
                else:
                    A_b4v = vk
            fns = []
            for c in range(8):
                bk, slot = kv_banks[c]
                ap = bank[bk][:, slot * 129:(slot + 1) * 129]
                fns.append(mm(ap, bk, ktmA[:, c * 128:(c + 1) * 128], vexA[:, c, :],
                              False, False))
            M(fns, (s_s, sk), (s_v, vk))

            rfns = []
            first_r = True
            for ic in range(8):
                for (lh, rh) in ((xnbA, wrhi), (xnbA, wrlo), (xnlA, wrhi)):
                    rfns.append(mm(bank[6][:, sti * 32:(sti + 1) * 32], 6,
                                   tok(lh, ic), rh[:, ic, :], first_r, False))
                    first_r = False
            grt = M(rfns, (s_v, vl), (s_s, A_b6))
            if sti == 1:
                A_grt1 = grt
            ss_ = S(lambda e_, sti=sti: e_.activation(scA, bank[6][:, sti * 32:(sti + 1) * 32],
                                                      AF.Sigmoid), (s_pe, grt), (s_v, A_sc))
            A_b6 = ss_
            A_sc = V(lambda e_: e_.tensor_tensor(sfcA, scA, rbb_s[:], ALU.add), (s_s, ss_))
            sfc3 = sfcA.rearrange("p (a b) -> p a b", b=8)
            t03 = t0A.rearrange("p (a b) -> p a b", b=8)
            y3 = yA.rearrange("p (a b) -> p a b", b=8)
            m1b = m1A.unsqueeze(2).to_broadcast([128, 4, 8])
            V(lambda e_, sfc3=sfc3: e_.tensor_reduce(m1A, sfc3, AX.X, ALU.max))
            V(lambda e_, sfc3=sfc3, m1b=m1b, t03=t03: e_.tensor_tensor(t03, sfc3, m1b,
                                                                       ALU.is_equal))
            V(lambda e_, t03=t03: e_.tensor_single_scalar(t0A, t0A, NEG, ALU.mult))
            V(lambda e_, sfc3=sfc3, t03=t03, y3=y3: e_.tensor_tensor(y3, sfc3, t03, ALU.add))
            V(lambda e_, y3=y3: e_.tensor_reduce(gmkA, y3, AX.X, ALU.max))
            V(lambda e_: e_.tensor_tensor(m1A, m1A, gmkA, ALU.add))
            V(lambda e_: e_.tensor_reduce(gm1A, m1A, AX.X, ALU.max))
            V(lambda e_: e_.tensor_scalar(gmkA, m1A, gm1A, NEG, ALU.is_equal, ALU.mult))
            V(lambda e_: e_.tensor_tensor(gmkA, m1A, gmkA, ALU.add))
            V(lambda e_: e_.tensor_reduce(gm2A, gmkA, AX.X, ALU.max))
            V(lambda e_: e_.tensor_single_scalar(gmkA, m1A, gm2A, ALU.is_ge))
            gmkb = gmkA.unsqueeze(2).to_broadcast([128, 4, 8])
            tmp3 = tmpA.rearrange("p (a b) -> p a b", b=8)
            V(lambda e_, gmkb=gmkb, tmp3=tmp3, sfc3=sfc3: e_.tensor_tensor(
                tmp3, sfc3, gmkb, ALU.mult))
            V(lambda e_: e_.tensor_single_scalar(posA, tmpA, 0.0, ALU.is_gt))
            V(lambda e_: e_.tensor_reduce(pA_, posA, AX.X, ALU.add))
            V(lambda e_: e_.memset(mx8A, NEG))
            V(lambda e_: e_.tensor_copy(cA[0], tmpA))
            for j in range(4):
                src_ = cA[j % 2]
                dst = cA[(j + 1) % 2]
                V(lambda e_, src_=src_: e_.reduce_max(mx8A[:, 0:1], src_, AX.X))
                if j < 3:
                    V(lambda e_, src_=src_, dst=dst: e_.match_replace(dst, mx8A, src_, NEG))
            V(lambda e_: e_.tensor_copy(t4A, mx8A[:, 0:1]))
            V(lambda e_: e_.tensor_single_scalar(mmainA, tmpA, t4A, ALU.is_ge))
            V(lambda e_: e_.tensor_single_scalar(zA, tmpA, 0.0, ALU.is_equal))
            V(lambda e_: e_.tensor_copy(pf[0], zA))
            for k2 in range(5):
                sh = 1 << k2
                src_ = pf[k2 % 2]
                dst = pf[(k2 + 1) % 2]
                V(lambda e_, src_=src_, dst=dst, sh=sh: e_.tensor_copy(dst[:, 0:sh],
                                                                       src_[:, 0:sh]))
                V(lambda e_, src_=src_, dst=dst, sh=sh: e_.tensor_tensor(
                    dst[:, sh:NE], src_[:, sh:NE], src_[:, 0:NE - sh], ALU.add))
            pfin = pf[1]
            V(lambda e_: e_.tensor_scalar(fmpA, pA_, -1.0, 4.0, ALU.mult, ALU.add))
            V(lambda e_, pfin=pfin: e_.tensor_single_scalar(pfin, pfin, fmpA, ALU.is_le))
            V(lambda e_, pfin=pfin: e_.tensor_tensor(zA, zA, pfin, ALU.mult))
            V(lambda e_: e_.tensor_tensor(zA, zA, posA, ALU.add))
            V(lambda e_: e_.tensor_single_scalar(cndA, pA_, 4.0, ALU.is_ge))
            V(lambda e_: e_.tensor_scalar(ucndA, cndA, -1.0, 1.0, ALU.mult, ALU.add))
            V(lambda e_: e_.tensor_single_scalar(mmainA, mmainA, cndA, ALU.mult))
            V(lambda e_: e_.tensor_single_scalar(zA, zA, ucndA, ALU.mult))
            V(lambda e_: e_.tensor_tensor(mmainA, mmainA, zA, ALU.add))
            V(lambda e_: e_.tensor_tensor(wnumA, scA, mmainA, ALU.mult))
            V(lambda e_: e_.tensor_reduce(dnmA, wnumA, AX.X, ALU.add))
            V(lambda e_: e_.tensor_single_scalar(dnmA, dnmA, 1e-20, ALU.add))
            V(lambda e_: e_.reciprocal(recA, dnmA))
            V(lambda e_: e_.tensor_scalar(wtA, wnumA, recA, 2.5, ALU.mult, ALU.mult))
            V(lambda e_, t=t, sti=sti: e_.tensor_copy(w_all[:, 2 * t + sti, :], wtA))

        if t + 1 < NT:
            emit_stats(t + 1, gb)

    b7bf = bank[7][:].bitcast(BF16)
    vwe = 0
    pw_all = s_v.n
    for sub in range(2 * NT):
        gtr = M([lambda e_, sub=sub: e_.transpose(b7bf[0:32, 0:128], w_all[:, sub, :], id_s[:])],
                (s_v, max(pw_all, vwe)))
        vwe = V(lambda e_, sub=sub: e_.tensor_copy(wemA, b7bf[0:32, 0:128]),
                (s_pe, gtr), (d_st, d_st.n))
        DT(lambda e_, sub=sub: e_.dma_start(
            out=w_scr[:, sub * 128:(sub + 1) * 128], in_=wemA), d_st, (s_v, vwe))

    first_ev = [(s_pe, s_pe.n)]
    for c in range(8):
        bk, slot = kv_banks[c]
        ap = bank[bk][:, slot * 129:(slot + 1) * 129]
        V(lambda e_, ap=ap, c=c: e_.tensor_copy(kvbd[0:64, c, 0:64], ap[0:64, 0:64]), *first_ev)
        first_ev = []
        V(lambda e_, ap=ap, c=c: e_.tensor_copy(kvbd[64:128, c, 64:128], ap[64:128, 64:128]))
        V(lambda e_, ap=ap, c=c: e_.tensor_copy(ksbd[0:64, c, 0:1], ap[0:64, 128:129]))
        V(lambda e_, ap=ap, c=c: e_.tensor_copy(ksbd[64:128, c, 1:2], ap[64:128, 128:129]))
    cbar()

    gw_unit = {}
    ld_row = {}
    pe_unit_done = {}

    def emit_unit_loads(qq, uu_, same_q=True):
        gi = qq * NU + uu_
        wpar = gi % 2
        waits = []
        if gi >= 2:
            key = (qq, uu_ - 2) if uu_ >= 2 else (qq - 1, NU - 2 + uu_)
            waits.append((s_pe, pe_unit_done[key]))
        DG(lambda e_, uu_=uu_, wpar=wpar: e_.dma_start(
            out=egv[wpar], in_=gw_h[uu_].rearrange("(c p) o -> p c o", p=128)),
            d_we[wpar], *waits)
        DG(lambda e_, uu_=uu_, wpar=wpar: e_.dma_start(
            out=euv[wpar], in_=uw_h[uu_].rearrange("(c p) o -> p c o", p=128)), d_we[wpar])
        gw_unit[gi] = DG(lambda e_, uu_=uu_, wpar=wpar: e_.dma_start(
            out=edv[wpar], in_=dw_h[uu_].rearrange("(c p) o -> p c o", p=128)), d_we[wpar])
        if uu_ < NE and same_q:
            ld_row[(qq, uu_)] = DL(lambda e_, qq=qq, uu_=uu_, wpar=wpar: e_.dma_start(
                out=wbB[wpar][:],
                in_=w_scr[uu_:uu_ + 1, qq * 1024:(qq + 1) * 1024].to_broadcast([128, 1024])),
                d_wb[wpar], (d_st, d_st.n), (s_v, s_v.n), *waits)

    emit_unit_loads(0, 0)
    emit_unit_loads(0, 1)
    lagB = {"gate": {0: 0, 1: 0, 6: 0}, "up": {2: 0, 3: 0, 7: 0}, "down": {4: 0, 5: 0},
            "sg": {0: 0, 1: 0}, "ev": {0: 0, 1: 0}, "hv": {0: 0, 1: 0}}

    for q in range(NQ):
        ldxq = DL(lambda e_, q=q: e_.dma_start(
            out=xq_bf[:],
            in_=xn_scr[:, q * 1024:(q + 1) * 1024].rearrange("(c p) t -> p c t", p=128)),
            d_xq, (d_st, d_st.n), (s_v, s_v.n), (s_pe, s_pe.n))
        for u0 in (0, 1):
            if (q, u0) not in ld_row:
                ld_row[(q, u0)] = DL(lambda e_, q=q, u0=u0: e_.dma_start(
                    out=wbB[u0 % 2][:],
                    in_=w_scr[u0:u0 + 1, q * 1024:(q + 1) * 1024].to_broadcast([128, 1024])),
                    d_wb[u0 % 2], (d_st, d_st.n), (s_v, s_v.n), (s_pe, s_pe.n),
                    (s_s, s_s.n), (s_p, s_p.n))

        for u in range(NU):
            gi = q * NU + u
            wpar = gi % 2
            xq_wait = [(d_xq, ldxq)] if u == 0 else []


            for tt in range(2):
                hvp = hvB[tt]
                hv_ms = 0
                for ic in range(4):
                    gbk = [0, 1, 6][(tt * 4 + ic) % 3]
                    ubk = [2, 3, 7][(tt * 4 + ic) % 3]
                    gg = M([mm(bank[gbk][:], gbk, egv[wpar][:, kc, ic * 128:(ic + 1) * 128],
                               xq_bf[:, kc, tt * 512:(tt + 1) * 512], kc == 0, kc == 7)
                            for kc in range(8)],
                           (d_we[wpar], gw_unit[gi]), (s_s, lagB["gate"][gbk]),
                           *(xq_wait if (u == 0 and tt == 0 and ic == 0) else []))
                    sg_ = S(lambda e_, gbk=gbk, ic=ic: e_.activation(sgB[ic % 2], bank[gbk][:],
                                                                    AF.Silu),
                            (s_pe, gg), (s_v, lagB["sg"][ic % 2]))
                    lagB["gate"][gbk] = sg_
                    gu = M([mm(bank[ubk][:], ubk, euv[wpar][:, kc, ic * 128:(ic + 1) * 128],
                               xq_bf[:, kc, tt * 512:(tt + 1) * 512], kc == 0, kc == 7)
                            for kc in range(8)], (s_v, lagB["up"][ubk]))
                    war_hv = [(s_pe, lagB["hv"][tt])] if ic == 0 else []
                    if u < NE:
                        vh1 = V(lambda e_, ubk=ubk, ic=ic: e_.tensor_tensor(
                            hv1B, bank[ubk][:], sgB[ic % 2], ALU.mult), (s_pe, gu), (s_s, sg_))
                        lagB["up"][ubk] = vh1
                        lagB["sg"][ic % 2] = vh1
                        hv_ms = V(lambda e_, hvp=hvp, ic=ic, wpar=wpar, tt=tt: e_.tensor_tensor(
                            hvp[:, ic, :], hv1B, wbB[wpar][:, tt * 512:(tt + 1) * 512],
                            ALU.mult), (d_wb[wpar], ld_row[(q, u)]), *war_hv)
                    else:
                        hv_ms = V(lambda e_, hvp=hvp, ic=ic, ubk=ubk: e_.tensor_tensor(
                            hvp[:, ic, :], bank[ubk][:], sgB[ic % 2], ALU.mult),
                            (s_pe, gu), (s_s, sg_), *war_hv)
                        lagB["up"][ubk] = hv_ms
                        lagB["sg"][ic % 2] = hv_ms
                for hc in range(8):
                    dbk = 4 + hc % 2
                    dwaits = [(s_s, lagB["down"][dbk])]
                    if hc == 0:
                        dwaits.append((s_v, hv_ms))
                    gdm = M([mm(bank[dbk][:], dbk, edv[wpar][:, ic, hc * 128:(hc + 1) * 128],
                                hvp[:, ic, :], ic == 0, ic == 3) for ic in range(4)], *dwaits)
                    se_ = S(lambda e_, dbk=dbk, hc=hc: e_.activation(evB[hc % 2], bank[dbk][:],
                                                                    AF.Copy),
                            (s_pe, gdm), (s_p, lagB["ev"][hc % 2]))
                    lagB["down"][dbk] = se_
                    if u == 0:
                        ph = P(lambda e_, hc=hc, tt=tt: e_.tensor_copy(
                            mixed[:, hc, tt * 512:(tt + 1) * 512], evB[hc % 2]), (s_s, se_))
                    else:
                        ph = P(lambda e_, hc=hc, tt=tt: e_.tensor_tensor(
                            mixed[:, hc, tt * 512:(tt + 1) * 512], evB[hc % 2],
                            mixed[:, hc, tt * 512:(tt + 1) * 512], ALU.add), (s_s, se_))
                    lagB["ev"][hc % 2] = ph
                lagB["hv"][tt] = s_pe.n
            pe_unit_done[(q, u)] = s_pe.n
            nq2, nu2 = (q, u + 2) if u + 2 < NU else (q + 1, u + 2 - NU)
            if nq2 < NQ:
                emit_unit_loads(nq2, nu2, same_q=(nq2 == q))

        spool = s_p.n
        for hc in range(8):
            S(lambda e_, hc=hc: e_.activation(gq_bf[:, hc, :], mixed[:, hc, :], AF.Silu),
              (s_p, spool))
        cbar()

        lagC = {"num": {0: 0, 1: 0}, "den": {4: 0, 6: 0}, "rd": {5: 0, 7: 0},
                "fc": {2: 0, 3: 0}, "pv2": 0, "h2": 0}
        qt_free = {0: 0, 1: 0}
        ldq_pre = {}
        ldq_pre[0] = DL(lambda e_, q=q: e_.dma_start(
            out=qtC[0][:],
            in_=q_scr[:, (q * 4) * 256:(q * 4 + 1) * 256].rearrange("(c p) t -> p c t", p=128)),
            d_qt[0], (d_st, d_st.n), (s_v, s_v.n), (s_pe, s_pe.n), (s_s, s_s.n), (s_p, s_p.n))
        ldq_pre[1] = DL(lambda e_, q=q: e_.dma_start(
            out=qtC[1][:],
            in_=q_scr[:, (q * 4 + 1) * 256:(q * 4 + 2) * 256].rearrange("(c p) t -> p c t", p=128)),
            d_qt[1], (d_st, d_st.n), (s_v, s_v.n), (s_pe, s_pe.n), (s_s, s_s.n), (s_p, s_p.n))
        for ct in range(4):
            gt = q * 4 + ct
            par = ct % 2
            ldq_ = ldq_pre[ct]
            ldxc = DL(lambda e_, gt=gt: e_.dma_start(
                out=xC[:],
                in_=x_h[:, gt * 256:(gt + 1) * 256].rearrange("(c p) t -> p c t", p=128)),
                d_xc, (s_v, s_v.n), (s_s, s_s.n), (s_p, s_p.n))
            for c in range(8):
                nbk = c % 2
                dnb = 4 + 2 * (c % 2)
                rdb = 5 + 2 * (c % 2)
                gn = M([mm(bank[nbk][:, 0:256], nbk, kvbd[:, c, :], qtC[par][:, c, :],
                           True, True)], (d_qt[par], ldq_), (s_v, lagC["num"][nbk]))
                gdn = M([mm(bank[dnb][0:2, 0:256], dnb, ksbd[:, c, :], qtC[par][:, c, :],
                            True, True)], (s_v, lagC["den"][dnb]))
                vr0 = V(lambda e_, dnb=dnb: e_.tensor_single_scalar(
                    rdC, bank[dnb][0:2, 0:256], 1e-8, ALU.add), (s_pe, gdn))
                lagC["den"][dnb] = vr0
                vr = V(lambda e_: e_.reciprocal(rd2C, rdC))
                gb_ = M([mm(bank[rdb][:, 0:256], rdb, e2_s[:], rd2C, True, True)],
                        (s_v, max(vr, lagC["rd"][rdb])))
                vc1 = V(lambda e_, rdb=rdb: e_.tensor_copy(rdbC, bank[rdb][:, 0:256]), (s_pe, gb_))
                lagC["rd"][rdb] = vc1
                nd_last = V(lambda e_, c=c, nbk=nbk: e_.tensor_tensor(
                    ndC[:, c, :], bank[nbk][:, 0:256], rdbC, ALU.mult), (s_pe, gn))
                lagC["num"][nbk] = nd_last
            qt_free[par] = s_pe.n
            if ct + 2 < 4:
                ldq_pre[ct + 2] = DL(lambda e_, gt=gt, par=par: e_.dma_start(
                    out=qtC[par][:],
                    in_=q_scr[:, (gt + 2) * 256:(gt + 3) * 256].rearrange("(c p) t -> p c t", p=128)),
                    d_qt[par], (s_pe, qt_free[par]))
            for oc in range(8):
                fbk = 2 + oc % 2
                gf = M([mm(bank[fbk][:, 0:256], fbk, wfc_s[:, ic, oc * 128:(oc + 1) * 128],
                           ndC[:, ic, :], ic == 0, ic == 7) for ic in range(8)],
                       (s_v, max(lagC["fc"][fbk], nd_last)))
                exf = [(s_p, lagC["pv2"])] if oc == 0 else []
                lagC["fc"][fbk] = V(lambda e_, oc=oc, fbk=fbk, ct=ct: e_.tensor_tensor(
                    xatC[:, oc, :], bank[fbk][:, 0:256],
                    xq_bf[:, oc, ct * 256:(ct + 1) * 256], ALU.add), (s_pe, gf), *exf)
            pv2 = P(lambda e_: e_.tensor_copy(xabC[:], xatC[:]), (s_v, s_v.n),
                    (s_pe, lagC["h2"]))
            lagC["pv2"] = pv2
            v2b = V(lambda e_: e_.tensor_tensor(xsqC[:], xabC[:], xabC[:], ALU.mult), (s_p, pv2))
            bank_reset(6)
            h1 = M([mm(bank[6][0:1, 0:256], 6, onesb[:], xabC[:, c, :], c == 0, c == 7)
                    for c in range(8)], (s_p, pv2), (s_v, lagC["den"][6]))
            h2 = M([mm(bank[6][0:1, 256:512], 6, onesb[:], xsqC[:, c, :], c == 0, c == 7)
                    for c in range(8)], (s_v, v2b))
            lagC["h2"] = h2
            V(lambda e_: e_.tensor_single_scalar(slC[0], bank[6][0:1, 0:256], 1.0 / D, ALU.mult),
              (s_pe, h2))
            V(lambda e_: e_.tensor_single_scalar(slC[1], bank[6][0:1, 256:512], 1.0 / D, ALU.mult))
            V(lambda e_: e_.tensor_tensor(slC[2], slC[0], slC[0], ALU.mult))
            V(lambda e_: e_.scalar_tensor_tensor(slC[3], slC[2], -1.0, slC[1],
                                                 ALU.mult, ALU.add))
            vv2 = V(lambda e_: e_.tensor_single_scalar(slC[1], slC[3], 1e-5, ALU.add))
            sv2 = S(lambda e_: e_.activation(slC[2], slC[1], AF.Sqrt), (s_v, vv2))
            V(lambda e_: e_.reciprocal(slC[3], slC[2]), (s_s, sv2))
            vb2 = V(lambda e_: e_.scalar_tensor_tensor(slC[1], slC[0], -1.0, slC[3],
                                                       ALU.mult, ALU.mult))
            bank_reset(6)
            ga2 = M([mm(bank[6][:, 0:256], 6, onesr[:], slC[3], True, True)], (s_v, vb2))
            gb2 = M([mm(bank[6][:, 256:512], 6, onesr[:], slC[1], True, True)])
            ab3 = bank[6][:, 0:256].unsqueeze(1).to_broadcast([128, 8, 256])
            bb3 = bank[6][:, 256:512].unsqueeze(1).to_broadcast([128, 8, 256])
            V(lambda e_, ab3=ab3: e_.tensor_tensor(t1C[:], xatC[:], ab3, ALU.mult), (s_pe, gb2))
            lagC["den"][6] = V(lambda e_, bb3=bb3: e_.tensor_tensor(t2C[:], t1C[:], bb3,
                                                                    ALU.add))
            for oc in range(8):
                V(lambda e_, oc=oc: e_.tensor_scalar(sC[:, oc, :], t2C[:, oc, :],
                                                     gam_s[:, oc:oc + 1], bet_s[:, oc:oc + 1],
                                                     ALU.mult, ALU.add))
            V(lambda e_, ct=ct: e_.tensor_tensor(
                p1C[:], sC[:], gq_bf[:, :, ct * 256:(ct + 1) * 256], ALU.mult))
            vo_ = V(lambda e_: e_.tensor_tensor(outC[:], p1C[:], xC[:], ALU.add),
                    (d_xc, ldxc), (d_out, d_out.n))
            DT(lambda e_, gt=gt: e_.dma_start(
                out=out_h[:, gt * 256:(gt + 1) * 256].rearrange("(c p) t -> p c t", p=128),
                in_=outC[:]), d_out, (s_v, vo_))
            if ct == 3:
                cbar()

    all_dma = [d_out, d_cn, d_wi, d_xc, d_qt[0], d_qt[1], d_xq, d_st,
               d_wb[0], d_wb[1], d_we[0], d_we[1], d_x[0], d_x[1], d_xb[0], d_xb[1]]

    def g_final(e_, finals=tuple((s.h, s.n) for s in all_dma)):
        for h, v in finals:
            e_.wait_ge(h, v)
    b.q["sync"].append(g_final)

    blk = e(nc.Block())
    for eng_name in ("sync", "gpsimd", "tensor", "scalar", "vector"):
        ops = b.q[eng_name]

        def body(engine, ops=ops):
            for f in ops:
                f(engine)
        getattr(blk, eng_name)(body)
    stack.close()
    return nc


def _prep(inputs):
    x = np.asarray(inputs["L0_attn_inputs"], dtype=np.float32)
    B = x.shape[0]
    x_fm = np.ascontiguousarray(x.transpose(0, 2, 1))
    f32 = lambda a: np.ascontiguousarray(np.asarray(a, dtype=np.float32))
    gw_t = np.asarray(inputs["L0_ffn_gate_w"], dtype=np.float32).transpose(0, 2, 1)
    uw_t = np.asarray(inputs["L0_ffn_up_w"], dtype=np.float32).transpose(0, 2, 1)
    dw_t = np.asarray(inputs["L0_ffn_down_w"], dtype=np.float32).transpose(0, 2, 1)
    sg_t = np.asarray(inputs["L0_ffn_shared_gate_w"], dtype=np.float32).T
    su_t = np.asarray(inputs["L0_ffn_shared_up_w"], dtype=np.float32).T
    sd_t = np.asarray(inputs["L0_ffn_shared_down_w"], dtype=np.float32).T
    sg2 = sg_t.reshape(D, 2, I_HALF).transpose(1, 0, 2)
    su2 = su_t.reshape(D, 2, I_HALF).transpose(1, 0, 2)
    sd2 = sd_t.reshape(2, I_HALF, D)
    e2 = np.zeros((2, 128), dtype=np.float32)
    e2[0, 0:64] = 1.0
    e2[1, 64:128] = 1.0
    shared = {
        "wq": f32(inputs["L0_attn_W_Q"]),
        "wk": f32(inputs["L0_attn_W_K"]),
        "wv": f32(inputs["L0_attn_W_V"]),
        "wfc": f32(inputs["L0_attn_W_fc"]),
        "wr_t": f32(np.asarray(inputs["L0_ffn_router_weight"], dtype=np.float32).T),
        "gamma2": f32(np.asarray(inputs["L0_attn_gamma"], dtype=np.float32).reshape(C8, 128).T),
        "beta2": f32(np.asarray(inputs["L0_attn_beta"], dtype=np.float32).reshape(C8, 128).T),
        "rbias": f32(np.asarray(inputs["L0_ffn_router_bias"], dtype=np.float32).reshape(1, NE)),
        "gw_all": f32(np.concatenate([gw_t, sg2], axis=0)),
        "uw_all": f32(np.concatenate([uw_t, su2], axis=0)),
        "dw_all": f32(np.concatenate([dw_t, sd2], axis=0)),
        "ones_col": np.ones((128, 1), dtype=ml_dtypes.bfloat16),
        "ones_row": np.ones((1, 128), dtype=np.float32),
        "ones_row_bf": np.ones((1, 128), dtype=ml_dtypes.bfloat16),
        "id128": np.eye(128, dtype=ml_dtypes.bfloat16),
        "e2": e2,
    }
    in_maps = []
    for bi in range(B):
        m = dict(shared)
        m["x_fm"] = x_fm[bi]
        in_maps.append(m)
    return in_maps


def kernel(**inputs):
    in_maps = _prep(inputs)
    T = in_maps[0]["x_fm"].shape[1]
    nc = build_core(T)
    res = run_bass_kernel_spmd(nc, in_maps, core_ids=list(range(len(in_maps))))
    outs = [np.asarray(r["out_fm"], dtype=np.float32).T for r in res.results]
    return np.ascontiguousarray(np.stack(outs, axis=0))
